# version 11 (final)
# baseline (speedup 1.0000x reference)
import math
from contextlib import ExitStack

import numpy as np
import ml_dtypes

import concourse.bass as bass
import concourse.mybir as mybir
from concourse.bass_utils import run_bass_kernel_spmd

F32 = mybir.dt.float32
BF16 = mybir.dt.bfloat16
AF = mybir.ActivationFunctionType
ALU = mybir.AluOpType

NCORES = 8
B = 2
T = 8192
D = 2048
HID = 8192
NH = 16
NKV = 4
HD = 128
TOK = 2048
TT = 1024
NDC = D // 128
NHC = HID // 128
INV_SQRT_HD = 1.0 / math.sqrt(HD)
EPS = 1e-5


class Emit:
    def __init__(self, nc, ctx, names):
        self.nc = nc
        self.h = {}
        self.c = {}
        for n in names:
            self.h[n] = ctx.enter_context(nc.semaphore(n))
            self.c[n] = 0
        self.prev_ffn_store = 0
        self.pending_tail = None
        self.wc_gu = False
        self.wc_dn = False
        self._uid = 0

    def u(self, base):
        self._uid += 1
        return f"{base}_{self._uid}"

    def dma(self, eng, out, in_, sem):
        eng.dma_start(out=out, in_=in_).then_inc(self.h[sem], 16)
        self.c[sem] += 16
        return self.c[sem]

    def inc(self, inst, sem):
        inst.then_inc(self.h[sem], 1)
        self.c[sem] += 1
        return self.c[sem]

    def wait(self, eng, sem, val):
        if val > 0:
            eng.wait_ge(self.h[sem], val)

    def store_fence(self):
        nc = self.nc
        for eng in (nc.sync, nc.gpsimd, nc.vector, nc.scalar):
            self.wait(eng, "st", self.c["st"])


def emit_stats(E, ctx, sb, x_src):
    nc = E.nc
    E.store_fence()
    with ExitStack() as ps:
        e = ps.enter_context
        xstage = [e(nc.sbuf_tensor(E.u("st_x"), [128, TOK], F32)) for i in range(4)]
        xb = [e(nc.sbuf_tensor(E.u("st_xb"), [128, TOK], BF16)) for i in range(2)]
        sqb = [e(nc.sbuf_tensor(E.u("st_sq"), [128, TOK], BF16)) for i in range(2)]
        rows = [e(nc.sbuf_tensor(E.u("st_r"), [1, TOK], F32)) for i in range(7)]
        murow, exrow, mu2row, varrow, veps, sqrow, nmurow = rows
        psx = e(nc.psum_tensor(E.u("st_psx"), [1, TOK], F32))
        psx2 = e(nc.psum_tensor(E.u("st_psx2"), [1, TOK], F32))
        ld_done = [0] * NDC
        cast_done = [0] * NDC
        mm_done = [0] * NDC
        for dc in range(NDC):
            par4 = dc % 4
            par2 = dc % 2
            if dc >= 4:
                E.wait(nc.sync, "pe", mm_done[dc - 4])
                E.wait(nc.sync, "ac", cast_done[dc - 4])
            ld_done[dc] = E.dma(
                nc.sync, xstage[par4][:], x_src.ap()[128 * dc:128 * (dc + 1), :],
                f"xs{par4}")
            E.wait(nc.scalar, f"xs{par4}", ld_done[dc])
            E.wait(nc.vector, f"xs{par4}", ld_done[dc])
            if dc >= 2:
                E.wait(nc.scalar, "pe", mm_done[dc - 2])
                E.wait(nc.vector, "pe", mm_done[dc - 2])
            E.inc(nc.vector.tensor_copy(xb[par2][:], xstage[par4][:]), "dv")
            cast_done[dc] = E.inc(
                nc.scalar.activation(out=sqb[par2][:], in_=xstage[par4][:],
                                     func=AF.Square), "ac")
            E.wait(nc.tensor, "ac", cast_done[dc])
            E.wait(nc.tensor, "dv", E.c["dv"])
            for tb in range(4):
                nc.tensor.matmul(
                    psx[0:1, 512 * tb:512 * (tb + 1)], sb["ones_b"][:],
                    xb[par2][:, 512 * tb:512 * (tb + 1)],
                    start=(dc == 0), stop=(dc == NDC - 1))
            for tb in range(4):
                inst = nc.tensor.matmul(
                    psx2[0:1, 512 * tb:512 * (tb + 1)], sb["ones_b"][:],
                    sqb[par2][:, 512 * tb:512 * (tb + 1)],
                    start=(dc == 0), stop=(dc == NDC - 1))
            mm_done[dc] = E.inc(inst, "pe")
        E.wait(nc.vector, "pe", mm_done[NDC - 1])
        nc.vector.tensor_scalar_mul(murow[:], psx[0:1, :], 1.0 / D)
        nc.vector.tensor_scalar_mul(exrow[:], psx2[0:1, :], 1.0 / D)
        nc.vector.tensor_mul(mu2row[:], murow[:], murow[:])
        nc.vector.tensor_sub(varrow[:], exrow[:], mu2row[:])
        v1 = E.inc(nc.vector.tensor_scalar_add(veps[:], varrow[:], EPS), "dv")
        E.wait(nc.scalar, "dv", v1)
        a1 = E.inc(nc.scalar.activation(out=sqrow[:], in_=veps[:],
                                        func=AF.Sqrt), "ac")
        E.wait(nc.vector, "ac", a1)
        nc.vector.reciprocal(sb["arow"][0:1, :], sqrow[:])
        nc.vector.tensor_scalar_mul(nmurow[:], murow[:], -1.0)
        E.inc(nc.vector.tensor_mul(sb["brow"][0:1, :], sb["arow"][0:1, :],
                                   nmurow[:]), "dv")
    nc.all_engine_barrier()


def emit_bcast_ab(E, sb, tt, absb):
    nc = E.nc
    E.wait(nc.scalar, "dv", E.c["dv"])
    with ExitStack() as ps:
        pa = ps.enter_context(nc.psum_tensor(E.u("bc_pa"), [128, 512], F32))
        a1 = 0
        for which, row in ((0, sb["arow"]), (1, sb["brow"])):
            for j in range(2):
                if a1:
                    E.wait(nc.tensor, "ac", a1)
                inst = nc.tensor.matmul(
                    pa[:], sb["ones_r"][:],
                    row[0:1, TT * tt + 512 * j:TT * tt + 512 * (j + 1)],
                    start=True, stop=True)
                p1 = E.inc(inst, "pe")
                E.wait(nc.scalar, "pe", p1)
                a1 = E.inc(nc.scalar.copy(
                    absb[:, TT * which + 512 * j:TT * which + 512 * (j + 1)],
                    pa[:]), "ac")
    return a1


def emit_xn(E, ctx, sb, x_src, tt, xn, absb_ready, absb):
    nc = E.nc
    dv0 = E.c["dv"]
    st0 = E.c["st"]
    with ExitStack() as ps:
        e = ps.enter_context
        xstage = [e(nc.sbuf_tensor(E.u("xn_x"), [128, TT], F32)) for i in range(3)]
        tmp = [e(nc.sbuf_tensor(E.u("xn_t"), [128, TT], F32)) for i in range(2)]
        xn_done = [0] * NDC
        for dc in range(NDC):
            par = dc % 3
            if dc >= 3:
                E.wait(nc.sync, "dv", xn_done[dc - 3])
            else:
                E.wait(nc.sync, "dv", dv0)
                E.wait(nc.sync, "st", st0)
            ld = E.dma(nc.sync, xstage[par][:],
                       x_src.ap()[128 * dc:128 * (dc + 1), TT * tt:TT * (tt + 1)],
                       f"xs{par}")
            E.wait(nc.vector, f"xs{par}", ld)
            if dc == 0:
                E.wait(nc.vector, "ac", absb_ready)
            nc.vector.tensor_mul(tmp[dc % 2][:], xstage[par][:], absb[:, 0:TT])
            xn_done[dc] = E.inc(
                nc.vector.tensor_add(xn[:, TT * dc:TT * (dc + 1)], tmp[dc % 2][:],
                                     absb[:, TT:2 * TT]), "dv")
    return xn_done


def emit_ffn(E, ctx, sb, x_src, x_dst, wgt, wut, wdt, caches, dbg=None):
    cache_g, cache_u, cache_d = caches
    """Full FFN sublayer: x_dst = x_src + 0.5 * FFN(LN(x_src)).
    Stats (arow/brow) must already be computed for x_src."""
    nc = E.nc
    for tt in range(2):
        E.store_fence()
        with ExitStack() as ph:
            e = ph.enter_context
            gated = e(nc.sbuf_tensor(E.u("ffn_g"), [128, NHC * TT], BF16))
            pha = ph.enter_context(ExitStack())
            xn = pha.enter_context(nc.sbuf_tensor(E.u("ffn_xn"), [128, NDC * TT], BF16))
            with ExitStack() as pa:
                absb = pa.enter_context(nc.sbuf_tensor(E.u("ffn_ab"), [128, 2 * TT], F32))
                ab_ready = emit_bcast_ab(E, sb, tt, absb)
                xn_ready = emit_xn(E, ctx, sb, x_src, tt, xn, ab_ready, absb)[NDC - 1]
            with ExitStack() as pa:
                gsl = [e2 for e2 in (pa.enter_context(nc.sbuf_tensor(E.u("ffn_gsl"), [128, NDC * 128], BF16)) for i in range(2))]
                usl = [e2 for e2 in (pa.enter_context(nc.sbuf_tensor(E.u("ffn_usl"), [128, NDC * 128], BF16)) for i in range(2))]
                silu = [pa.enter_context(nc.sbuf_tensor(E.u("ffn_sl"), [128, TT], F32)) for i in range(2)]
                psg = [pa.enter_context(nc.psum_tensor(E.u("ffn_psg"), [128, TT], F32)) for i in range(2)]
                psu = [pa.enter_context(nc.psum_tensor(E.u("ffn_psu"), [128, TT], F32)) for i in range(2)]
                wld = [0] * NHC
                gate_done = [0] * NHC
                up_done = [0] * NHC
                silu_done = [0] * NHC
                gated_done = [0] * NHC
                for hc in range(NHC):
                    par = hc % 2
                    ldeng = nc.scalar if E.wc_gu else nc.gpsimd
                    if hc >= 2:
                        E.wait(ldeng, "pe", up_done[hc - 2])
                        if not E.wc_gu:
                            E.wait(ldeng, f"wc{par}", E.c[f"wc{par}"])
                    else:
                        E.wait(ldeng, "dv", xn_ready)
                    if E.wc_gu:
                        if hc == 0:
                            for p_ in range(4):
                                E.wait(ldeng, f"wc{p_}", E.c[f"wc{p_}"])
                        for cch, slab in ((cache_g, gsl[par]), (cache_u, usl[par])):
                            wld[hc] = E.dma(ldeng, slab[:], cch.ap()[hc],
                                            f"wg{par}")
                    else:
                        for w_dram, slab in ((wgt, gsl[par]), (wut, usl[par])):
                            for half in range(2):
                                wld[hc] = E.dma(
                                    nc.gpsimd,
                                    slab[:].rearrange("p (dc m) -> p dc m", dc=NDC)[:, 8 * half:8 * (half + 1)],
                                    w_dram.ap()[:, 128 * hc:128 * (hc + 1)]
                                    .rearrange("(dc p) m -> p dc m", p=128)[:, 8 * half:8 * (half + 1)],
                                    f"wg{par}")
                        E.wait(nc.sync, f"wg{par}", wld[hc])
                        E.dma(nc.sync, cache_g.ap()[hc], gsl[par][:], f"wc{par}")
                        E.dma(nc.sync, cache_u.ap()[hc], usl[par][:], f"wc{par}")
                    E.wait(nc.tensor, f"wg{par}", wld[hc])
                    if hc == 0:
                        E.wait(nc.tensor, "dv", xn_ready)
                    if hc >= 2:
                        E.wait(nc.tensor, "ac", silu_done[hc - 2])
                        E.wait(nc.tensor, "dv", gated_done[hc - 2])
                    for dc in range(NDC):
                        for tb in range(2):
                            inst = nc.tensor.matmul(
                                psg[par][:, 512 * tb:512 * (tb + 1)],
                                gsl[par][:, 128 * dc:128 * (dc + 1)],
                                xn[:, TT * dc + 512 * tb:TT * dc + 512 * (tb + 1)],
                                start=(dc == 0), stop=(dc == NDC - 1))
                    gate_done[hc] = E.inc(inst, "pe")
                    for dc in range(NDC):
                        for tb in range(2):
                            inst = nc.tensor.matmul(
                                psu[par][:, 512 * tb:512 * (tb + 1)],
                                usl[par][:, 128 * dc:128 * (dc + 1)],
                                xn[:, TT * dc + 512 * tb:TT * dc + 512 * (tb + 1)],
                                start=(dc == 0), stop=(dc == NDC - 1))
                    up_done[hc] = E.inc(inst, "pe")
                    E.wait(nc.scalar, "pe", gate_done[hc])
                    if hc >= 2:
                        E.wait(nc.scalar, "dv", gated_done[hc - 2])
                    silu_done[hc] = E.inc(
                        nc.scalar.activation(out=silu[par][:], in_=psg[par][:],
                                             func=AF.Silu), "ac")
                    E.wait(nc.vector, "pe", up_done[hc])
                    E.wait(nc.vector, "ac", silu_done[hc])
                    gated_done[hc] = E.inc(
                        nc.vector.tensor_mul(gated[:, TT * hc:TT * (hc + 1)],
                                             psu[par][:], silu[par][:]), "dv")
                a2_pe = up_done[NHC - 1]
                a2_dv = gated_done[NHC - 1]
            pha.close()
            with ExitStack() as pb:
                dwn = [pb.enter_context(nc.sbuf_tensor(E.u("ffn_dw"), [128, 2 * 512], BF16)) for i in range(4)]
                xold = pb.enter_context(nc.sbuf_tensor(E.u("ffn_xo"), [128, 8 * 512], F32))
                xnew = pb.enter_context(nc.sbuf_tensor(E.u("ffn_xw"), [128, 8 * 512], F32))
                psd = [pb.enter_context(nc.psum_tensor(E.u("ffn_psd"), [128, 512], F32)) for i in range(8)]
                evac_done = [0] * 4
                store_done = [0] * 4
                xold_ld = [0] * 4
                prev_dn = [a2_pe] * 4
                E.wait(nc.sync, "pe", a2_pe)
                E.wait(nc.sync, "dv", a2_dv)
                E.wait(nc.gpsimd, "dv", a2_dv)
                for dg in range(4):
                    if dg >= 1:
                        E.wait(nc.sync, "dv", evac_done[dg - 1])
                    for ddl in range(4):
                        dd = 4 * dg + ddl
                        xold_ld[dg] = E.dma(
                            nc.sync,
                            xold[:, 1024 * ddl:1024 * (ddl + 1)],
                            x_src.ap()[128 * dd:128 * (dd + 1), TT * tt:TT * (tt + 1)],
                            "xs0")
                    dn_done = [0] * NHC
                    for hp in range(NHC // 2):
                        par = hp % 4
                        ldeng = nc.scalar if E.wc_dn else nc.gpsimd
                        E.wait(ldeng, "pe", prev_dn[par])
                        if not E.wc_dn:
                            E.wait(ldeng, f"wc{par}", E.c[f"wc{par}"])
                        if E.wc_dn:
                            if dg == 0 and hp == 0:
                                for p_ in range(4):
                                    E.wait(ldeng, f"wc{p_}", E.c[f"wc{p_}"])
                            wl = E.dma(ldeng, dwn[par][:],
                                       cache_d.ap()[dg, hp], f"wg{par}")
                        else:
                            wl = E.dma(
                                nc.gpsimd,
                                dwn[par][:].rearrange("p (two d) -> p two d", two=2),
                                wdt.ap()[256 * hp:256 * (hp + 1), 512 * dg:512 * (dg + 1)]
                                .rearrange("(two p) d -> p two d", p=128),
                                f"wg{par}")
                            E.wait(nc.sync, f"wg{par}", wl)
                            E.dma(nc.sync, cache_d.ap()[dg, hp], dwn[par][:],
                                  f"wc{par}")
                        E.wait(nc.tensor, f"wg{par}", wl)
                        if hp == 0 and dg >= 1:
                            E.wait(nc.tensor, "dv", evac_done[dg - 1])
                        for two in range(2):
                            hc = 2 * hp + two
                            for ddl in range(4):
                                for tb in range(2):
                                    inst = nc.tensor.matmul(
                                        psd[2 * ddl + tb][:],
                                        dwn[par][:, 512 * two + 128 * ddl:512 * two + 128 * (ddl + 1)],
                                        gated[:, TT * hc + 512 * tb:TT * hc + 512 * (tb + 1)],
                                        start=(hc == 0), stop=(hc == NHC - 1))
                            dn_done[hc] = E.inc(inst, "pe")
                        prev_dn[par] = dn_done[2 * hp + 1]
                    E.wait(nc.vector, "pe", dn_done[NHC - 1])
                    E.wait(nc.vector, "xs0", xold_ld[dg])
                    E.wait(nc.vector, "st", E.c["st"])
                    for ddl in range(4):
                        for tb in range(2):
                            inst = nc.vector.scalar_tensor_tensor(
                                xnew[:, (2 * ddl + tb) * 512:(2 * ddl + tb + 1) * 512],
                                psd[2 * ddl + tb][:], 0.5,
                                xold[:, (2 * ddl + tb) * 512:(2 * ddl + tb + 1) * 512],
                                ALU.mult, ALU.add)
                    evac_done[dg] = E.inc(inst, "dv")
                    E.wait(nc.sync, "dv", evac_done[dg])
                    for ddl in range(4):
                        dd = 4 * dg + ddl
                        store_done[dg] = E.dma(
                            nc.sync,
                            x_dst.ap()[128 * dd:128 * (dd + 1), TT * tt:TT * (tt + 1)],
                            xnew[:, 1024 * ddl:1024 * (ddl + 1)],
                            "st")
                E.prev_ffn_store = store_done[3]
            E.wc_gu = True
            E.wc_dn = True
        nc.all_engine_barrier()


def emit_rope(E, nc, psq, cos, sin, t1, t2, dst_lo, dst_hi, width, tcol):
    cl = cos[0:64, tcol:tcol + width]
    ch = cos[64:128, tcol:tcol + width]
    sl = sin[0:64, tcol:tcol + width]
    sh = sin[64:128, tcol:tcol + width]
    nc.vector.tensor_mul(t1[:], psq[0:64, :], cl)
    nc.vector.tensor_mul(t2[:], psq[64:128, :], sh)
    nc.vector.tensor_sub(dst_lo, t1[:], t2[:])
    nc.vector.tensor_mul(t1[:], psq[64:128, :], ch)
    nc.vector.tensor_mul(t2[:], psq[0:64, :], sl)
    inst = nc.vector.tensor_add(dst_hi, t1[:], t2[:])
    return inst


def emit_qkv(E, ctx, sb, x1, wq, wk, wv, kloc, vloc, kall, vall, q_sb, cos, sin):
    nc = E.nc
    E.store_fence()
    with ExitStack() as ph:
        e = ph.enter_context
        kbuf = e(nc.sbuf_tensor(E.u("kv_k"), [128, NKV * TT], BF16))
        vtbuf = e(nc.sbuf_tensor(E.u("kv_v"), [128, 8 * 512], BF16))
        xn = e(nc.sbuf_tensor(E.u("kv_xn"), [128, NDC * TT], BF16))
        t1 = e(nc.sbuf_tensor(E.u("kv_t1"), [64, TT], F32))
        t2 = e(nc.sbuf_tensor(E.u("kv_t2"), [64, TT], F32))
        vstage = e(nc.sbuf_tensor(E.u("kv_vs"), [128, TT], BF16))
        slab = [e(nc.sbuf_tensor(E.u("kv_sl"), [128, NDC * 128], BF16)) for i in range(3)]
        psq = [e(nc.psum_tensor(E.u("kv_ps"), [128, TT], F32)) for i in range(3)]
        pst = e(nc.psum_tensor(E.u("kv_pst"), [128, TT], BF16))

        epi_done = [0, 0, 0]
        mm_done = [0, 0, 0]
        last_tr = 0
        last_vt = 0
        for tt in range(2):
            with ExitStack() as pa:
                absb = pa.enter_context(nc.sbuf_tensor(E.u("kv_ab"), [128, 2 * TT], F32))
                ab_ready = emit_bcast_ab(E, sb, tt, absb)
                E.wait(nc.vector, "pe", E.c["pe"])
                xn_ready = emit_xn(E, ctx, sb, x1, tt, xn, ab_ready, absb)[NDC - 1]
            units = ([("k", ic) for ic in range(NKV)]
                     + [("v", ic) for ic in range(NKV)]
                     + [("q", ic) for ic in range(NH)])
            for u, (kind, ic) in enumerate(units):
                if tt == 1:
                    u += len(units)
                if u % len(units) == 2 * NKV:
                    E.wait(nc.sync, "dv", E.c["dv"])
                    c1 = E.dma(nc.sync,
                               kloc[tt].ap().rearrange("(kh p) t -> p kh t", p=128),
                               kbuf[:].rearrange("p (kh t) -> p kh t", kh=NKV),
                               "st")
                    c2 = E.dma(nc.sync,
                               vloc[tt].ap().rearrange("(m p) f -> p m f", p=128),
                               vtbuf[:].rearrange("p (m f) -> p m f", m=8), "st")
                    E.wait(nc.gpsimd, "st", max(c1, c2))
                    E.inc(nc.gpsimd.collective_compute(
                        "AllGather", ALU.bypass,
                        replica_groups=[[0, 1, 2, 3], [4, 5, 6, 7]],
                        ins=[kloc[tt].ap().opt()], outs=[kall[tt].ap().opt()]), "cc")
                    E.inc(nc.gpsimd.collective_compute(
                        "AllGather", ALU.bypass,
                        replica_groups=[[0, 1, 2, 3], [4, 5, 6, 7]],
                        ins=[vloc[tt].ap().opt()], outs=[vall[tt].ap().opt()]), "cc")
                par = u % 3
                w_dram = {"k": wk, "v": wv, "q": wq}[kind]
                E.wait(nc.gpsimd, "pe", mm_done[par])
                if u % len(units) == 0:
                    E.wait(nc.gpsimd, "dv", xn_ready)
                for half in range(2):
                    wl = E.dma(
                        nc.gpsimd,
                        slab[par][:].rearrange("p (dc m) -> p dc m", dc=NDC)[:, 8 * half:8 * (half + 1)],
                        w_dram.ap()[:, 128 * ic:128 * (ic + 1)]
                        .rearrange("(dc p) m -> p dc m", p=128)[:, 8 * half:8 * (half + 1)],
                        f"wg{par}")
                E.wait(nc.tensor, f"wg{par}", wl)
                if u % len(units) == 0:
                    E.wait(nc.tensor, "dv", xn_ready)
                E.wait(nc.tensor, "dv", epi_done[par])
                for dc in range(NDC):
                    for tb in range(2):
                        inst = nc.tensor.matmul(
                            psq[par][:, 512 * tb:512 * (tb + 1)],
                            slab[par][:, 128 * dc:128 * (dc + 1)],
                            xn[:, TT * dc + 512 * tb:TT * dc + 512 * (tb + 1)],
                            start=(dc == 0), stop=(dc == NDC - 1))
                mm_done[par] = E.inc(inst, "pe")
                if kind == "k":
                    E.wait(nc.vector, "pe", mm_done[par])
                    if u % len(units) == 0 and tt == 1:
                        E.wait(nc.vector, "st", E.c["st"])
                    col = ic * TT
                    epi_done[par] = E.inc(
                        emit_rope(E, nc, psq[par], cos, sin, t1, t2,
                                  kbuf[0:64, col:col + TT],
                                  kbuf[64:128, col:col + TT], TT, TT * tt),
                        "dv")
                elif kind == "q":
                    E.wait(nc.vector, "pe", mm_done[par])
                    col = ic * TOK + TT * tt
                    epi_done[par] = E.inc(
                        emit_rope(E, nc, psq[par], cos, sin, t1, t2,
                                  q_sb[0:64, col:col + TT],
                                  q_sb[64:128, col:col + TT], TT, TT * tt),
                        "dv")
                else:
                    E.wait(nc.scalar, "pe", mm_done[par])
                    E.wait(nc.scalar, "pe", last_tr)
                    ac1 = E.inc(nc.scalar.copy(vstage[:], psq[par][:]), "ac")
                    E.wait(nc.tensor, "ac", ac1)
                    E.wait(nc.tensor, "dv", last_vt)
                    for m in range(8):
                        inst = nc.tensor.transpose(
                            pst[:, 128 * m:128 * (m + 1)],
                            vstage[:, 128 * m:128 * (m + 1)], sb["eye"][:])
                    last_tr = E.inc(inst, "pe")
                    E.wait(nc.vector, "pe", last_tr)
                    for m in range(8):
                        inst = nc.vector.tensor_copy(
                            vtbuf[:, m * 512 + ic * 128:m * 512 + (ic + 1) * 128],
                            pst[:, 128 * m:128 * (m + 1)])
                    last_vt = E.inc(inst, "dv")
                    epi_done[par] = last_vt
    nc.all_engine_barrier()
    return E.c["cc"]


def emit_attention(E, ctx, sb, kall, vall, q_sb, o_dram, masks_dram, cc_done):
    nc = E.nc
    E.store_fence()
    with ExitStack() as ph:
        e = ph.enter_context
        kfm = [e(nc.sbuf_tensor(E.u("at_k"), [128, 64 * 128], BF16)) for i in range(2)]
        vtm = [e(nc.sbuf_tensor(E.u("at_v"), [128, 64 * 128], BF16)) for i in range(2)]
        masks = e(nc.sbuf_tensor(E.u("at_m"), [128, 16 * 512], BF16))
        DEPTH = 4
        LA = 3
        ptile = [e(nc.sbuf_tensor(E.u("at_p"), [128, 512], BF16)) for i in range(DEPTH)]
        pt2 = [e(nc.sbuf_tensor(E.u("at_q"), [128, 512], BF16)) for i in range(DEPTH)]
        rrow = [e(nc.sbuf_tensor(E.u("at_r"), [1, 512], F32)) for i in range(2)]
        bcsb = [e(nc.sbuf_tensor(E.u("at_b"), [128, 512], F32)) for i in range(2)]
        onorm = [e(nc.sbuf_tensor(E.u("at_o"), [128, 512], BF16)) for i in range(2)]
        pss = [e(nc.psum_tensor(E.u("at_ps"), [128, 512], F32)) for i in range(DEPTH)]
        pso = [e(nc.psum_tensor(E.u("at_po"), [128, 512], F32)) for i in range(2)]
        psl = e(nc.psum_tensor(E.u("at_pl"), [1, 512], F32))
        psb = e(nc.psum_tensor(E.u("at_pb"), [128, 512], F32))

        mask_ld = E.dma(nc.sync, masks[:].rearrange("p (m n) -> p m n", m=16),
                        masks_dram.ap().rearrange("m p n -> p m n"), "xc")
        E.wait(nc.vector, "xc", mask_ld)

        blk = 0
        norm_done = [0, 0]
        store_done = [0, 0]
        recip_done_last = 0
        pb_done = [0, 0]
        bc_copy_done = 0
        kv_ld = [0] * NKV
        pe_end = [0] * NKV
        pe_last_kv = 0
        prev_exp_last = 0
        prev_pv_last = 0

        def load_kv(kh, gate_pe):
            sem = f"xs{kh % 2}"
            E.wait(nc.sync, "cc", cc_done)
            E.wait(nc.sync, "pe", gate_pe)
            for rp in range(4):
                for half in range(2):
                    E.dma(nc.sync,
                          kfm[kh % 2][:].rearrange("p (m rp d) -> p rp m d", rp=4, d=128)
                          [:, rp, 8 * half:8 * (half + 1)],
                          kall[half].ap()[512 * rp + 128 * kh:512 * rp + 128 * (kh + 1), :]
                          .rearrange("p (m d) -> p m d", d=128),
                          sem)
                    kv_ld[kh] = E.dma(
                        nc.sync,
                        vtm[kh % 2][:].rearrange("p (m rp d) -> p rp m d", rp=4, d=128)
                        [:, rp, 8 * half:8 * (half + 1)],
                        vall[half].ap()[:, 128 * kh:128 * (kh + 1)]
                        .rearrange("(rp m p) d -> p rp m d", rp=4, p=128)[:, rp],
                        sem)

        load_kv(0, 0)
        load_kv(1, 0)
        for kh in range(NKV):
            kfm_c = kfm[kh % 2]
            vtm_c = vtm[kh % 2]
            first_of_kh = [True]
            for qh4 in range(4):
                h = 4 * kh + qh4
                for a in range(4):
                    bp = blk % 2
                    nch = 16 * a + 16
                    exp_done = [0] * nch
                    s_done = [0] * nch
                    pv_done = [0] * nch
                    msk_done = [0] * nch

                    def emit_S(c, h=h, a=a, blk=blk, exp_done=exp_done,
                               s_done=s_done, pv_done=pv_done, msk_done=msk_done):
                        sp = c % DEPTH
                        if first_of_kh[0]:
                            E.wait(nc.tensor, f"xs{kh % 2}", kv_ld[kh])
                            first_of_kh[0] = False
                        if c >= DEPTH:
                            if c % 2 == 0:
                                E.wait(nc.tensor, "ac", exp_done[c - LA])
                        elif blk > 0 and c == 0:
                            E.wait(nc.tensor, "ac", prev_exp_last)
                        s_done[c] = E.inc(
                            nc.tensor.matmul(pss[sp][:],
                                             kfm_c[:, 128 * c:128 * (c + 1)],
                                             q_sb[:, h * TOK + 512 * a:h * TOK + 512 * (a + 1)],
                                             start=True, stop=True), "pe")
                        E.wait(nc.scalar, "pe", s_done[c])
                        exp_done[c] = E.inc(
                            nc.scalar.activation(out=ptile[sp][:], in_=pss[sp][:],
                                                 func=AF.Exp, scale=INV_SQRT_HD), "ac")
                        if c >= 16 * a:
                            mi = c - 16 * a
                            E.wait(nc.vector, "ac", exp_done[c])
                            if c >= DEPTH:
                                E.wait(nc.vector, "pe", pv_done[c - DEPTH])
                            elif blk > 0:
                                E.wait(nc.vector, "pe", prev_pv_last)
                            msk_done[c] = E.inc(
                                nc.vector.tensor_mul(pt2[sp][:], ptile[sp][:],
                                                     masks[:, 512 * mi:512 * (mi + 1)]),
                                "dv")

                    def emit_PV(c, a=a, bp=bp, nch=nch, exp_done=exp_done,
                                pv_done=pv_done, msk_done=msk_done):
                        sp = c % DEPTH
                        if c >= 16 * a:
                            if c % 2 == 0 or c == 16 * a:
                                E.wait(nc.tensor, "dv",
                                       msk_done[min(c + 1, nch - 1)]
                                       if c % 2 == 0 and c + 1 >= 16 * a and c + 1 < nch
                                       else msk_done[c])
                            rhs = pt2[sp]
                        else:
                            if c % 2 == 0:
                                E.wait(nc.tensor, "ac",
                                       exp_done[min(c + 1, 16 * a - 1, nch - 1)])
                            rhs = ptile[sp]
                        if c == 0:
                            E.wait(nc.tensor, "dv",
                                   max(norm_done[bp], recip_done_last))
                        nc.tensor.matmul(pso[bp][:], vtm_c[:, 128 * c:128 * (c + 1)],
                                         rhs[:], start=(c == 0), stop=(c == nch - 1))
                        pv_done[c] = E.inc(
                            nc.tensor.matmul(psl[0:1, :], sb["ones_b"][:],
                                             rhs[:], start=(c == 0), stop=(c == nch - 1)),
                            "pe")

                    for c in range(nch):
                        emit_S(c)
                        if c == LA - 1 and E.pending_tail is not None:
                            E.pending_tail()
                            E.pending_tail = None
                        if c >= LA:
                            emit_PV(c - LA)
                    for c in range(nch - LA, nch):
                        emit_PV(c)
                    prev_exp_last = exp_done[nch - 1]
                    prev_pv_last = pv_done[nch - 1]
                    pe_last_kv = pv_done[nch - 1]
                    pe_end[kh] = pv_done[nch - 1]
                    E.wait(nc.vector, "pe", pv_done[nch - 1])
                    E.wait(nc.vector, "pe", pb_done[bp])
                    recip_done_last = E.inc(
                        nc.vector.reciprocal(rrow[bp][:], psl[0:1, :]), "dv")
                    _rd = recip_done_last
                    _bp, _h, _a, _blk = bp, h, a, blk

                    def tail(bp=_bp, h=_h, a=_a, blk=_blk, rd=_rd):
                        nonlocal bc_copy_done
                        E.wait(nc.tensor, "dv", rd)
                        E.wait(nc.tensor, "ac", bc_copy_done)
                        pb1 = E.inc(
                            nc.tensor.matmul(psb[:], sb["ones_r"][:], rrow[bp][:],
                                             start=True, stop=True), "pe")
                        pb_done[bp] = pb1
                        E.wait(nc.scalar, "pe", pb1)
                        if blk >= 2:
                            E.wait(nc.scalar, "dv", norm_done[bp])
                        bc_copy_done = E.inc(nc.scalar.copy(bcsb[bp][:], psb[:]), "ac")
                        E.wait(nc.vector, "ac", bc_copy_done)
                        if blk >= 2:
                            E.wait(nc.vector, "st", E.c["st"])
                        norm_done[bp] = E.inc(
                            nc.vector.tensor_mul(onorm[bp][:], pso[bp][:],
                                                 bcsb[bp][:]), "dv")
                        E.wait(nc.sync, "dv", norm_done[bp])
                        store_done[bp] = E.dma(
                            nc.sync,
                            o_dram.ap()[128 * h:128 * (h + 1), 512 * a:512 * (a + 1)],
                            onorm[bp][:], "st")

                    E.pending_tail = tail
                    blk += 1
            if kh + 2 < NKV:
                load_kv(kh + 2, pe_end[kh])
        if E.pending_tail is not None:
            E.pending_tail()
            E.pending_tail = None
    nc.all_engine_barrier()


def emit_wo(E, ctx, sb, wo, o_dram, x1, x_dst):
    nc = E.nc
    E.store_fence()
    with ExitStack() as ph:
        e = ph.enter_context
        wo_sb = e(nc.sbuf_tensor(E.u("wo_w"), [128, NH * D], BF16))
        oa = [e(nc.sbuf_tensor(E.u("wo_o"), [128, NH * 512], BF16)) for i in range(2)]
        x1st = [e(nc.sbuf_tensor(E.u("wo_x"), [128, NDC * 512], F32)) for i in range(2)]
        x2sb = [e(nc.sbuf_tensor(E.u("wo_y"), [128, 512], F32)) for i in range(4)]
        psw = [e(nc.psum_tensor(E.u("wo_ps"), [128, 512], F32)) for i in range(4)]
        wl_ic = [0] * NH
        for ic in range(NH):
            wl_ic[ic] = E.dma(nc.gpsimd,
                              wo_sb[:, D * ic:D * (ic + 1)],
                              wo.ap()[128 * ic:128 * (ic + 1), :], "wg0")
        wl = wl_ic[NH - 1]
        ev_done = [0] * 64
        st_done = [0] * 64
        oa_ld = [0, 0]
        x1_ld = [0, 0]
        mm_done_last = [0, 0]
        ev_done_last = [0, 0]
        idx = 0
        for a in range(4):
            apar = a % 2
            if a >= 2:
                E.wait(nc.sync, "pe", mm_done_last[apar])
                E.wait(nc.sync, "dv", ev_done_last[apar])
            oa_ld[apar] = E.dma(
                nc.sync,
                oa[apar][:].rearrange("p (ic t) -> p ic t", ic=NH),
                o_dram.ap().rearrange("(ic p) t -> p ic t", p=128)[:, :, 512 * a:512 * (a + 1)],
                f"xs{apar}")
            x1_ld[apar] = E.dma(
                nc.sync,
                x1st[apar][:].rearrange("p (dc t) -> p dc t", dc=NDC),
                x1.ap().rearrange("(dc p) t -> p dc t", p=128)[:, :, 512 * a:512 * (a + 1)],
                f"xs{apar}")
            for dd in range(NDC):
                par = idx % 4
                if idx > 0:
                    E.wait(nc.tensor, "wg0", wl)
                E.wait(nc.tensor, f"xs{apar}", x1_ld[apar])
                if idx >= 4:
                    E.wait(nc.tensor, "dv", ev_done[idx - 4])
                for ic in range(NH):
                    if idx == 0:
                        E.wait(nc.tensor, "wg0", wl_ic[ic])
                    inst = nc.tensor.matmul(
                        psw[par][:],
                        wo_sb[:, D * ic + 128 * dd:D * ic + 128 * (dd + 1)],
                        oa[apar][:, 512 * ic:512 * (ic + 1)],
                        start=(ic == 0), stop=(ic == NH - 1))
                mm = E.inc(inst, "pe")
                mm_done_last[apar] = mm
                E.wait(nc.vector, "pe", mm)
                E.wait(nc.vector, f"xs{apar}", x1_ld[apar])
                if idx >= 4:
                    E.wait(nc.vector, "st", E.c["st"])
                ev_done[idx] = E.inc(
                    nc.vector.tensor_add(x2sb[par][:], psw[par][:],
                                         x1st[apar][:, 512 * dd:512 * (dd + 1)]),
                    "dv")
                ev_done_last[apar] = ev_done[idx]
                E.wait(nc.sync, "dv", ev_done[idx])
                st_done[idx] = E.dma(
                    nc.sync,
                    x_dst.ap()[128 * dd:128 * (dd + 1), 512 * a:512 * (a + 1)],
                    x2sb[par][:], "st")
                idx += 1
    nc.all_engine_barrier()


def build_program(stage="full"):
    nc = bass.Bass("TRN2", target_bir_lowering=False)

    xfm = nc.dram_tensor("xfm", [D, TOK], F32, kind="ExternalInput")
    wq = nc.dram_tensor("wq", [D, D], F32, kind="ExternalInput")
    wk = nc.dram_tensor("wk", [D, 512], F32, kind="ExternalInput")
    wv = nc.dram_tensor("wv", [D, 512], F32, kind="ExternalInput")
    wo = nc.dram_tensor("wo", [D, D], F32, kind="ExternalInput")
    wgt = nc.dram_tensor("wgt", [D, HID], F32, kind="ExternalInput")
    wut = nc.dram_tensor("wut", [D, HID], F32, kind="ExternalInput")
    wdt = nc.dram_tensor("wdt", [HID, D], F32, kind="ExternalInput")
    cost = nc.dram_tensor("cost", [128, TOK], F32, kind="ExternalInput")
    sint = nc.dram_tensor("sint", [128, TOK], F32, kind="ExternalInput")
    masks_dram = nc.dram_tensor("masks", [16, 128, 512], BF16, kind="ExternalInput")

    out = nc.dram_tensor("out", [D, TOK], F32, kind="ExternalOutput")

    cache_g = nc.dram_tensor("cwg", [NHC, 128, D], BF16, kind="Internal")
    cache_u = nc.dram_tensor("cwu", [NHC, 128, D], BF16, kind="Internal")
    cache_d = nc.dram_tensor("cwd", [4, NHC // 2, 128, 1024], BF16, kind="Internal")
    x1d = nc.dram_tensor("x1d", [D, TOK], F32, kind="Internal")
    x2d = nc.dram_tensor("x2d", [D, TOK], F32, kind="Internal")
    kloc = [nc.dram_tensor(f"kloc{i}", [512, TT], BF16, kind="Internal")
            for i in range(2)]
    vloc = [nc.dram_tensor(f"vloc{i}", [TT, 512], BF16, kind="Internal")
            for i in range(2)]
    kall = [nc.dram_tensor(f"kall{i}", [4 * 512, TT], BF16, kind="Internal")
            for i in range(2)]
    vall = [nc.dram_tensor(f"vall{i}", [4 * TT, 512], BF16, kind="Internal")
            for i in range(2)]
    o_dram = nc.dram_tensor("o_d", [D, TOK], BF16, kind="Internal")

    eye_np = np.eye(128, dtype=ml_dtypes.bfloat16)
    ones_f_np = np.ones((128, 1), dtype=np.float32)
    ones_b_np = np.ones((128, 1), dtype=ml_dtypes.bfloat16)
    ones_r_np = np.ones((1, 128), dtype=np.float32)
    eye_d = nc.inline_tensor(eye_np, "eye_c")
    ones_f_d = nc.inline_tensor(ones_f_np, "onesf_c")
    ones_b_d = nc.inline_tensor(ones_b_np, "onesb_c")
    ones_r_d = nc.inline_tensor(ones_r_np, "onesr_c")

    with ExitStack() as ctx:
        e = ctx.enter_context
        E = Emit(nc, ctx, ["wg0", "wg1", "wg2", "wg3",
                           "xs0", "xs1", "xs2", "xs3", "xc",
                           "wc0", "wc1", "wc2", "wc3",
                           "pe", "ac", "dv", "st", "cc"])
        sb = {}
        sb["eye"] = e(nc.sbuf_tensor(E.u("c_eye"), [128, 128], BF16))
        sb["ones_f"] = e(nc.sbuf_tensor(E.u("c_onef"), [128, 1], F32))
        sb["ones_b"] = e(nc.sbuf_tensor(E.u("c_oneb"), [128, 1], BF16))
        sb["ones_r"] = e(nc.sbuf_tensor(E.u("c_oner"), [1, 128], F32))
        sb["arow"] = e(nc.sbuf_tensor(E.u("c_arow"), [1, TOK], F32))
        sb["brow"] = e(nc.sbuf_tensor(E.u("c_brow"), [1, TOK], F32))


        E.dma(nc.sync, sb["eye"][:], eye_d.ap(), "xc")
        E.dma(nc.sync, sb["ones_f"][:], ones_f_d.ap(), "xc")
        E.dma(nc.sync, sb["ones_b"][:], ones_b_d.ap(), "xc")
        cl = E.dma(nc.sync, sb["ones_r"][:], ones_r_d.ap(), "xc")
        E.wait(nc.tensor, "xc", cl)
        E.wait(nc.vector, "xc", cl)
        E.wait(nc.scalar, "xc", cl)
        nc.all_engine_barrier()

        emit_stats(E, ctx, sb, xfm)
        ffn1_dst = out if stage == "ffn1" else x1d
        emit_ffn(E, ctx, sb, xfm, ffn1_dst, wgt, wut, wdt,
                 (cache_g, cache_u, cache_d))
        if stage == "ffn1":
            E.wait(nc.sync, "st", E.c["st"])
            return nc

        emit_stats(E, ctx, sb, x1d)
        with ExitStack() as big:
            q_sb = big.enter_context(nc.sbuf_tensor(E.u("q_sb"), [128, NH * TOK], BF16))
            with ExitStack() as qk:
                cos = qk.enter_context(nc.sbuf_tensor(E.u("c_cos"), [128, TOK], F32))
                sin = qk.enter_context(nc.sbuf_tensor(E.u("c_sin"), [128, TOK], F32))
                E.dma(nc.sync, cos[:], cost.ap(), "xc")
                csl = E.dma(nc.sync, sin[:], sint.ap(), "xc")
                E.wait(nc.vector, "xc", csl)
                cc_done = emit_qkv(E, ctx, sb, x1d, wq, wk, wv, kloc, vloc,
                                   kall, vall, q_sb, cos, sin)
            emit_attention(E, ctx, sb, kall, vall, q_sb, o_dram, masks_dram,
                           cc_done)
        wo_dst = out if stage == "x2" else x2d
        emit_wo(E, ctx, sb, wo, o_dram, x1d, wo_dst)
        if stage == "x2":
            E.wait(nc.sync, "st", E.c["st"])
            return nc

        emit_stats(E, ctx, sb, x2d)
        emit_ffn(E, ctx, sb, x2d, out, wgt, wut, wdt,
                 (cache_g, cache_u, cache_d))
        E.wait(nc.sync, "st", E.c["st"])
    return nc


def prepare_inmaps(inputs):
    x = np.ascontiguousarray(np.asarray(inputs["L0_attn_x"], dtype=np.float32))
    wq = np.ascontiguousarray(np.asarray(inputs["L0_attn_Wq"], dtype=np.float32))
    wk = np.ascontiguousarray(np.asarray(inputs["L0_attn_Wk"], dtype=np.float32))
    wv = np.ascontiguousarray(np.asarray(inputs["L0_attn_Wv"], dtype=np.float32))
    wo = np.ascontiguousarray(np.asarray(inputs["L0_attn_Wo"], dtype=np.float32))
    wgt = np.ascontiguousarray(np.asarray(inputs["L0_ffn_w_gate"], dtype=np.float32).T)
    wut = np.ascontiguousarray(np.asarray(inputs["L0_ffn_w_up"], dtype=np.float32).T)
    wdt = np.ascontiguousarray(np.asarray(inputs["L0_ffn_w_down"], dtype=np.float32).T)

    inv = (1.0 / 10000.0 ** (np.arange(0, HD, 2, dtype=np.float64) / HD))

    in_maps = []
    idx_all = []
    for c in range(NCORES):
        b, r = c // 4, c % 4
        m = np.arange(16)
        i = np.arange(128)
        tpos = (128 * (4 * m[:, None] + r) + i[None, :]).reshape(-1)
        idx_all.append((b, tpos))
        xfm = np.ascontiguousarray(x[b][tpos].T)
        f = tpos[None, :].astype(np.float64) * inv[:, None]
        cos_t = np.concatenate([np.cos(f), np.cos(f)], axis=0).astype(np.float32)
        sin_t = np.concatenate([np.sin(f), np.sin(f)], axis=0).astype(np.float32)
        masks = np.zeros((16, 128, 512), dtype=np.float32)
        stair = (np.arange(128)[None, :] >= np.arange(128)[:, None]).astype(np.float32)
        for mi in range(16):
            for s in range(4):
                g = 4 * s + r
                if mi < g:
                    masks[mi, :, 128 * s:128 * (s + 1)] = 1.0
                elif mi == g:
                    masks[mi, :, 128 * s:128 * (s + 1)] = stair
        in_maps.append({
            "xfm": xfm, "wq": wq, "wk": wk, "wv": wv, "wo": wo,
            "wgt": wgt, "wut": wut, "wdt": wdt,
            "cost": cos_t, "sint": sin_t,
            "masks": masks.astype(ml_dtypes.bfloat16),
        })
    return in_maps, idx_all


def kernel(**inputs):
    in_maps, idx_all = prepare_inmaps(inputs)
    nc = build_program("full")
    res = run_bass_kernel_spmd(nc, in_maps, core_ids=list(range(NCORES)))
    out = np.empty((B, T, D), dtype=np.float32)
    for c in range(NCORES):
        b, tpos = idx_all[c]
        out[b][tpos] = np.asarray(res.results[c]["out"], dtype=np.float32).T
    return out


# version 12
# speedup vs baseline: 1.0041x; 1.0041x over previous
import math
from contextlib import ExitStack

import numpy as np
import ml_dtypes

import concourse.bass as bass
import concourse.mybir as mybir
from concourse.bass_utils import run_bass_kernel_spmd

F32 = mybir.dt.float32
BF16 = mybir.dt.bfloat16
AF = mybir.ActivationFunctionType
ALU = mybir.AluOpType

NCORES = 8
B = 2
T = 8192
D = 2048
HID = 8192
NH = 16
NKV = 4
HD = 128
TOK = 2048
TT = 1024
NDC = D // 128
NHC = HID // 128
INV_SQRT_HD = 1.0 / math.sqrt(HD)
EPS = 1e-5


class Emit:
    def __init__(self, nc, ctx, names):
        self.nc = nc
        self.h = {}
        self.c = {}
        for n in names:
            self.h[n] = ctx.enter_context(nc.semaphore(n))
            self.c[n] = 0
        self.prev_ffn_store = 0
        self.pending_tail = None
        self.wc_gu = False
        self.wc_dn = False
        self._uid = 0

    def u(self, base):
        self._uid += 1
        return f"{base}_{self._uid}"

    def dma(self, eng, out, in_, sem):
        eng.dma_start(out=out, in_=in_).then_inc(self.h[sem], 16)
        self.c[sem] += 16
        return self.c[sem]

    def inc(self, inst, sem):
        inst.then_inc(self.h[sem], 1)
        self.c[sem] += 1
        return self.c[sem]

    def wait(self, eng, sem, val):
        if val > 0:
            eng.wait_ge(self.h[sem], val)

    def store_fence(self):
        nc = self.nc
        for eng in (nc.sync, nc.gpsimd, nc.vector, nc.scalar):
            self.wait(eng, "st", self.c["st"])


def emit_stats(E, ctx, sb, x_src):
    nc = E.nc
    E.store_fence()
    with ExitStack() as ps:
        e = ps.enter_context
        xstage = [e(nc.sbuf_tensor(E.u("st_x"), [128, TOK], F32)) for i in range(4)]
        xb = [e(nc.sbuf_tensor(E.u("st_xb"), [128, TOK], BF16)) for i in range(2)]
        sqb = [e(nc.sbuf_tensor(E.u("st_sq"), [128, TOK], BF16)) for i in range(2)]
        rows = [e(nc.sbuf_tensor(E.u("st_r"), [1, TOK], F32)) for i in range(7)]
        murow, exrow, mu2row, varrow, veps, sqrow, nmurow = rows
        psx = e(nc.psum_tensor(E.u("st_psx"), [1, TOK], F32))
        psx2 = e(nc.psum_tensor(E.u("st_psx2"), [1, TOK], F32))
        ld_done = [0] * NDC
        cast_done = [0] * NDC
        mm_done = [0] * NDC
        for dc in range(NDC):
            par4 = dc % 4
            par2 = dc % 2
            if dc >= 4:
                E.wait(nc.sync, "pe", mm_done[dc - 4])
                E.wait(nc.sync, "ac", cast_done[dc - 4])
            ld_done[dc] = E.dma(
                nc.sync, xstage[par4][:], x_src.ap()[128 * dc:128 * (dc + 1), :],
                f"xs{par4}")
            E.wait(nc.scalar, f"xs{par4}", ld_done[dc])
            E.wait(nc.vector, f"xs{par4}", ld_done[dc])
            if dc >= 2:
                E.wait(nc.scalar, "pe", mm_done[dc - 2])
                E.wait(nc.vector, "pe", mm_done[dc - 2])
            E.inc(nc.vector.tensor_copy(xb[par2][:], xstage[par4][:]), "dv")
            cast_done[dc] = E.inc(
                nc.scalar.activation(out=sqb[par2][:], in_=xstage[par4][:],
                                     func=AF.Square), "ac")
            E.wait(nc.tensor, "ac", cast_done[dc])
            E.wait(nc.tensor, "dv", E.c["dv"])
            for tb in range(4):
                nc.tensor.matmul(
                    psx[0:1, 512 * tb:512 * (tb + 1)], sb["ones_b"][:],
                    xb[par2][:, 512 * tb:512 * (tb + 1)],
                    start=(dc == 0), stop=(dc == NDC - 1))
            for tb in range(4):
                inst = nc.tensor.matmul(
                    psx2[0:1, 512 * tb:512 * (tb + 1)], sb["ones_b"][:],
                    sqb[par2][:, 512 * tb:512 * (tb + 1)],
                    start=(dc == 0), stop=(dc == NDC - 1))
            mm_done[dc] = E.inc(inst, "pe")
        E.wait(nc.vector, "pe", mm_done[NDC - 1])
        nc.vector.tensor_scalar_mul(murow[:], psx[0:1, :], 1.0 / D)
        nc.vector.tensor_scalar_mul(exrow[:], psx2[0:1, :], 1.0 / D)
        nc.vector.tensor_mul(mu2row[:], murow[:], murow[:])
        nc.vector.tensor_sub(varrow[:], exrow[:], mu2row[:])
        v1 = E.inc(nc.vector.tensor_scalar_add(veps[:], varrow[:], EPS), "dv")
        E.wait(nc.scalar, "dv", v1)
        a1 = E.inc(nc.scalar.activation(out=sqrow[:], in_=veps[:],
                                        func=AF.Sqrt), "ac")
        E.wait(nc.vector, "ac", a1)
        nc.vector.reciprocal(sb["arow"][0:1, :], sqrow[:])
        nc.vector.tensor_scalar_mul(nmurow[:], murow[:], -1.0)
        E.inc(nc.vector.tensor_mul(sb["brow"][0:1, :], sb["arow"][0:1, :],
                                   nmurow[:]), "dv")
    nc.all_engine_barrier()


def emit_bcast_ab(E, sb, tt, absb):
    nc = E.nc
    E.wait(nc.scalar, "dv", E.c["dv"])
    with ExitStack() as ps:
        pa = ps.enter_context(nc.psum_tensor(E.u("bc_pa"), [128, 512], F32))
        a1 = 0
        for which, row in ((0, sb["arow"]), (1, sb["brow"])):
            for j in range(2):
                if a1:
                    E.wait(nc.tensor, "ac", a1)
                inst = nc.tensor.matmul(
                    pa[:], sb["ones_r"][:],
                    row[0:1, TT * tt + 512 * j:TT * tt + 512 * (j + 1)],
                    start=True, stop=True)
                p1 = E.inc(inst, "pe")
                E.wait(nc.scalar, "pe", p1)
                a1 = E.inc(nc.scalar.copy(
                    absb[:, TT * which + 512 * j:TT * which + 512 * (j + 1)],
                    pa[:]), "ac")
    return a1


def emit_xn(E, ctx, sb, x_src, tt, xn, absb_ready, absb):
    nc = E.nc
    dv0 = E.c["dv"]
    st0 = E.c["st"]
    with ExitStack() as ps:
        e = ps.enter_context
        xstage = [e(nc.sbuf_tensor(E.u("xn_x"), [128, TT], F32)) for i in range(3)]
        tmp = [e(nc.sbuf_tensor(E.u("xn_t"), [128, TT], F32)) for i in range(2)]
        xn_done = [0] * NDC
        for dc in range(NDC):
            par = dc % 3
            if dc >= 3:
                E.wait(nc.sync, "dv", xn_done[dc - 3])
            else:
                E.wait(nc.sync, "dv", dv0)
                E.wait(nc.sync, "st", st0)
            ld = E.dma(nc.sync, xstage[par][:],
                       x_src.ap()[128 * dc:128 * (dc + 1), TT * tt:TT * (tt + 1)],
                       f"xs{par}")
            E.wait(nc.vector, f"xs{par}", ld)
            if dc == 0:
                E.wait(nc.vector, "ac", absb_ready)
            nc.vector.tensor_mul(tmp[dc % 2][:], xstage[par][:], absb[:, 0:TT])
            xn_done[dc] = E.inc(
                nc.vector.tensor_add(xn[:, TT * dc:TT * (dc + 1)], tmp[dc % 2][:],
                                     absb[:, TT:2 * TT]), "dv")
    return xn_done


def emit_ffn(E, ctx, sb, x_src, x_dst, wgt, wut, wdt, caches, dbg=None):
    cache_g, cache_u, cache_d = caches
    """Full FFN sublayer: x_dst = x_src + 0.5 * FFN(LN(x_src)).
    Stats (arow/brow) must already be computed for x_src."""
    nc = E.nc
    for tt in range(2):
        E.store_fence()
        with ExitStack() as ph:
            e = ph.enter_context
            gated = e(nc.sbuf_tensor(E.u("ffn_g"), [128, NHC * TT], BF16))
            pha = ph.enter_context(ExitStack())
            xn = pha.enter_context(nc.sbuf_tensor(E.u("ffn_xn"), [128, NDC * TT], BF16))
            with ExitStack() as pa:
                absb = pa.enter_context(nc.sbuf_tensor(E.u("ffn_ab"), [128, 2 * TT], F32))
                ab_ready = emit_bcast_ab(E, sb, tt, absb)
                xn_ready = emit_xn(E, ctx, sb, x_src, tt, xn, ab_ready, absb)[NDC - 1]
            with ExitStack() as pa:
                gsl = [e2 for e2 in (pa.enter_context(nc.sbuf_tensor(E.u("ffn_gsl"), [128, NDC * 128], BF16)) for i in range(2))]
                usl = [e2 for e2 in (pa.enter_context(nc.sbuf_tensor(E.u("ffn_usl"), [128, NDC * 128], BF16)) for i in range(2))]
                silu = [pa.enter_context(nc.sbuf_tensor(E.u("ffn_sl"), [128, TT], F32)) for i in range(2)]
                psg = [pa.enter_context(nc.psum_tensor(E.u("ffn_psg"), [128, TT], F32)) for i in range(2)]
                psu = [pa.enter_context(nc.psum_tensor(E.u("ffn_psu"), [128, TT], F32)) for i in range(2)]
                wld = [0] * NHC
                gate_done = [0] * NHC
                up_done = [0] * NHC
                silu_done = [0] * NHC
                gated_done = [0] * NHC
                for hc in range(NHC):
                    par = hc % 2
                    ldeng = nc.scalar if E.wc_gu else nc.gpsimd
                    if hc >= 2:
                        E.wait(ldeng, "pe", up_done[hc - 2])
                        if not E.wc_gu:
                            E.wait(ldeng, f"wc{par}", E.c[f"wc{par}"])
                    else:
                        E.wait(ldeng, "dv", xn_ready)
                    if E.wc_gu:
                        if hc == 0:
                            for p_ in range(4):
                                E.wait(ldeng, f"wc{p_}", E.c[f"wc{p_}"])
                        for cch, slab in ((cache_g, gsl[par]), (cache_u, usl[par])):
                            wld[hc] = E.dma(ldeng, slab[:], cch.ap()[hc],
                                            f"wg{par}")
                    else:
                        for w_dram, slab in ((wgt, gsl[par]), (wut, usl[par])):
                            for half in range(2):
                                wld[hc] = E.dma(
                                    nc.gpsimd,
                                    slab[:].rearrange("p (dc m) -> p dc m", dc=NDC)[:, 8 * half:8 * (half + 1)],
                                    w_dram.ap()[:, 128 * hc:128 * (hc + 1)]
                                    .rearrange("(dc p) m -> p dc m", p=128)[:, 8 * half:8 * (half + 1)],
                                    f"wg{par}")
                        E.wait(nc.sync, f"wg{par}", wld[hc])
                        E.dma(nc.sync, cache_g.ap()[hc], gsl[par][:], f"wc{par}")
                        E.dma(nc.sync, cache_u.ap()[hc], usl[par][:], f"wc{par}")
                    E.wait(nc.tensor, f"wg{par}", wld[hc])
                    if hc == 0:
                        E.wait(nc.tensor, "dv", xn_ready)
                    if hc >= 2:
                        E.wait(nc.tensor, "ac", silu_done[hc - 2])
                        E.wait(nc.tensor, "dv", gated_done[hc - 2])
                    for dc in range(NDC):
                        for tb in range(2):
                            inst = nc.tensor.matmul(
                                psg[par][:, 512 * tb:512 * (tb + 1)],
                                gsl[par][:, 128 * dc:128 * (dc + 1)],
                                xn[:, TT * dc + 512 * tb:TT * dc + 512 * (tb + 1)],
                                start=(dc == 0), stop=(dc == NDC - 1))
                    gate_done[hc] = E.inc(inst, "pe")
                    for dc in range(NDC):
                        for tb in range(2):
                            inst = nc.tensor.matmul(
                                psu[par][:, 512 * tb:512 * (tb + 1)],
                                usl[par][:, 128 * dc:128 * (dc + 1)],
                                xn[:, TT * dc + 512 * tb:TT * dc + 512 * (tb + 1)],
                                start=(dc == 0), stop=(dc == NDC - 1))
                    up_done[hc] = E.inc(inst, "pe")
                    E.wait(nc.scalar, "pe", gate_done[hc])
                    if hc >= 2:
                        E.wait(nc.scalar, "dv", gated_done[hc - 2])
                    silu_done[hc] = E.inc(
                        nc.scalar.activation(out=silu[par][:], in_=psg[par][:],
                                             func=AF.Silu), "ac")
                    E.wait(nc.vector, "pe", up_done[hc])
                    E.wait(nc.vector, "ac", silu_done[hc])
                    gated_done[hc] = E.inc(
                        nc.vector.tensor_mul(gated[:, TT * hc:TT * (hc + 1)],
                                             psu[par][:], silu[par][:]), "dv")
                a2_pe = up_done[NHC - 1]
                a2_dv = gated_done[NHC - 1]
            pha.close()
            with ExitStack() as pb:
                dwn = [pb.enter_context(nc.sbuf_tensor(E.u("ffn_dw"), [128, 2 * 512], BF16)) for i in range(4)]
                xold = pb.enter_context(nc.sbuf_tensor(E.u("ffn_xo"), [128, 8 * 512], F32))
                xnew = pb.enter_context(nc.sbuf_tensor(E.u("ffn_xw"), [128, 8 * 512], F32))
                psd = [pb.enter_context(nc.psum_tensor(E.u("ffn_psd"), [128, 512], F32)) for i in range(8)]
                evac_done = [0] * 4
                store_done = [0] * 4
                xold_ld = [0] * 4
                prev_dn = [a2_pe] * 4
                E.wait(nc.sync, "pe", a2_pe)
                E.wait(nc.sync, "dv", a2_dv)
                E.wait(nc.gpsimd, "dv", a2_dv)
                for dg in range(4):
                    if dg >= 1:
                        E.wait(nc.sync, "dv", evac_done[dg - 1])
                    for ddl in range(4):
                        dd = 4 * dg + ddl
                        xold_ld[dg] = E.dma(
                            nc.sync,
                            xold[:, 1024 * ddl:1024 * (ddl + 1)],
                            x_src.ap()[128 * dd:128 * (dd + 1), TT * tt:TT * (tt + 1)],
                            "xs0")
                    dn_done = [0] * NHC
                    for hp in range(NHC // 2):
                        par = hp % 4
                        ldeng = nc.scalar if E.wc_dn else nc.gpsimd
                        E.wait(ldeng, "pe", prev_dn[par])
                        if not E.wc_dn:
                            E.wait(ldeng, f"wc{par}", E.c[f"wc{par}"])
                        if E.wc_dn:
                            if dg == 0 and hp == 0:
                                for p_ in range(4):
                                    E.wait(ldeng, f"wc{p_}", E.c[f"wc{p_}"])
                            wl = E.dma(ldeng, dwn[par][:],
                                       cache_d.ap()[dg, hp], f"wg{par}")
                        else:
                            wl = E.dma(
                                nc.gpsimd,
                                dwn[par][:].rearrange("p (two d) -> p two d", two=2),
                                wdt.ap()[256 * hp:256 * (hp + 1), 512 * dg:512 * (dg + 1)]
                                .rearrange("(two p) d -> p two d", p=128),
                                f"wg{par}")
                            E.wait(nc.sync, f"wg{par}", wl)
                            E.dma(nc.sync, cache_d.ap()[dg, hp], dwn[par][:],
                                  f"wc{par}")
                        E.wait(nc.tensor, f"wg{par}", wl)
                        if hp == 0 and dg >= 1:
                            E.wait(nc.tensor, "dv", evac_done[dg - 1])
                        for two in range(2):
                            hc = 2 * hp + two
                            for ddl in range(4):
                                for tb in range(2):
                                    inst = nc.tensor.matmul(
                                        psd[2 * ddl + tb][:],
                                        dwn[par][:, 512 * two + 128 * ddl:512 * two + 128 * (ddl + 1)],
                                        gated[:, TT * hc + 512 * tb:TT * hc + 512 * (tb + 1)],
                                        start=(hc == 0), stop=(hc == NHC - 1))
                            dn_done[hc] = E.inc(inst, "pe")
                        prev_dn[par] = dn_done[2 * hp + 1]
                    E.wait(nc.vector, "pe", dn_done[NHC - 1])
                    E.wait(nc.vector, "xs0", xold_ld[dg])
                    E.wait(nc.vector, "st", E.c["st"])
                    for ddl in range(4):
                        for tb in range(2):
                            inst = nc.vector.scalar_tensor_tensor(
                                xnew[:, (2 * ddl + tb) * 512:(2 * ddl + tb + 1) * 512],
                                psd[2 * ddl + tb][:], 0.5,
                                xold[:, (2 * ddl + tb) * 512:(2 * ddl + tb + 1) * 512],
                                ALU.mult, ALU.add)
                    evac_done[dg] = E.inc(inst, "dv")
                    E.wait(nc.sync, "dv", evac_done[dg])
                    for ddl in range(4):
                        dd = 4 * dg + ddl
                        store_done[dg] = E.dma(
                            nc.sync,
                            x_dst.ap()[128 * dd:128 * (dd + 1), TT * tt:TT * (tt + 1)],
                            xnew[:, 1024 * ddl:1024 * (ddl + 1)],
                            "st")
                E.prev_ffn_store = store_done[3]
            E.wc_gu = True
            E.wc_dn = True
        nc.all_engine_barrier()


def emit_rope(E, nc, psq, cos, sin, t1, t2, dst_lo, dst_hi, width, tcol):
    cl = cos[0:64, tcol:tcol + width]
    ch = cos[64:128, tcol:tcol + width]
    sl = sin[0:64, tcol:tcol + width]
    sh = sin[64:128, tcol:tcol + width]
    nc.vector.tensor_mul(t1[:], psq[0:64, :], cl)
    nc.vector.tensor_mul(t2[:], psq[64:128, :], sh)
    nc.vector.tensor_sub(dst_lo, t1[:], t2[:])
    nc.vector.tensor_mul(t1[:], psq[64:128, :], ch)
    nc.vector.tensor_mul(t2[:], psq[0:64, :], sl)
    inst = nc.vector.tensor_add(dst_hi, t1[:], t2[:])
    return inst


def emit_qkv(E, ctx, sb, x1, wq, wk, wv, kloc, vloc, kall, vall, q_sb, cos, sin):
    nc = E.nc
    E.store_fence()
    with ExitStack() as ph:
        e = ph.enter_context
        kbuf = e(nc.sbuf_tensor(E.u("kv_k"), [128, NKV * TT], BF16))
        vtbuf = e(nc.sbuf_tensor(E.u("kv_v"), [128, 8 * 512], BF16))
        xn = e(nc.sbuf_tensor(E.u("kv_xn"), [128, NDC * TT], BF16))
        t1 = e(nc.sbuf_tensor(E.u("kv_t1"), [64, TT], F32))
        t2 = e(nc.sbuf_tensor(E.u("kv_t2"), [64, TT], F32))
        vstage = e(nc.sbuf_tensor(E.u("kv_vs"), [128, TT], BF16))
        slab = [e(nc.sbuf_tensor(E.u("kv_sl"), [128, NDC * 128], BF16)) for i in range(3)]
        psq = [e(nc.psum_tensor(E.u("kv_ps"), [128, TT], F32)) for i in range(3)]
        pst = e(nc.psum_tensor(E.u("kv_pst"), [128, TT], BF16))

        epi_done = [0, 0, 0]
        mm_done = [0, 0, 0]
        last_tr = 0
        last_vt = 0
        for tt in range(2):
            with ExitStack() as pa:
                absb = pa.enter_context(nc.sbuf_tensor(E.u("kv_ab"), [128, 2 * TT], F32))
                ab_ready = emit_bcast_ab(E, sb, tt, absb)
                E.wait(nc.vector, "pe", E.c["pe"])
                xn_ready = emit_xn(E, ctx, sb, x1, tt, xn, ab_ready, absb)[NDC - 1]
            units = ([("k", ic) for ic in range(NKV)]
                     + [("v", ic) for ic in range(NKV)]
                     + [("q", ic) for ic in range(NH)])
            for u, (kind, ic) in enumerate(units):
                if tt == 1:
                    u += len(units)
                if u % len(units) == 2 * NKV:
                    E.wait(nc.sync, "dv", E.c["dv"])
                    c1 = E.dma(nc.sync,
                               kloc[tt].ap().rearrange("(kh p) t -> p kh t", p=128),
                               kbuf[:].rearrange("p (kh t) -> p kh t", kh=NKV),
                               "st")
                    c2 = E.dma(nc.sync,
                               vloc[tt].ap().rearrange("(m p) f -> p m f", p=128),
                               vtbuf[:].rearrange("p (m f) -> p m f", m=8), "st")
                    E.wait(nc.gpsimd, "st", max(c1, c2))
                    E.inc(nc.gpsimd.collective_compute(
                        "AllGather", ALU.bypass,
                        replica_groups=[[0, 1, 2, 3], [4, 5, 6, 7]],
                        ins=[kloc[tt].ap().opt()], outs=[kall[tt].ap().opt()]), "cc")
                    E.inc(nc.gpsimd.collective_compute(
                        "AllGather", ALU.bypass,
                        replica_groups=[[0, 1, 2, 3], [4, 5, 6, 7]],
                        ins=[vloc[tt].ap().opt()], outs=[vall[tt].ap().opt()]), "cc")
                par = u % 3
                w_dram = {"k": wk, "v": wv, "q": wq}[kind]
                E.wait(nc.gpsimd, "pe", mm_done[par])
                if u % len(units) == 0:
                    E.wait(nc.gpsimd, "dv", xn_ready)
                for half in range(2):
                    wl = E.dma(
                        nc.gpsimd,
                        slab[par][:].rearrange("p (dc m) -> p dc m", dc=NDC)[:, 8 * half:8 * (half + 1)],
                        w_dram.ap()[:, 128 * ic:128 * (ic + 1)]
                        .rearrange("(dc p) m -> p dc m", p=128)[:, 8 * half:8 * (half + 1)],
                        f"wg{par}")
                E.wait(nc.tensor, f"wg{par}", wl)
                if u % len(units) == 0:
                    E.wait(nc.tensor, "dv", xn_ready)
                E.wait(nc.tensor, "dv", epi_done[par])
                for dc in range(NDC):
                    for tb in range(2):
                        inst = nc.tensor.matmul(
                            psq[par][:, 512 * tb:512 * (tb + 1)],
                            slab[par][:, 128 * dc:128 * (dc + 1)],
                            xn[:, TT * dc + 512 * tb:TT * dc + 512 * (tb + 1)],
                            start=(dc == 0), stop=(dc == NDC - 1))
                mm_done[par] = E.inc(inst, "pe")
                if kind == "k":
                    E.wait(nc.vector, "pe", mm_done[par])
                    if u % len(units) == 0 and tt == 1:
                        E.wait(nc.vector, "st", E.c["st"])
                    col = ic * TT
                    epi_done[par] = E.inc(
                        emit_rope(E, nc, psq[par], cos, sin, t1, t2,
                                  kbuf[0:64, col:col + TT],
                                  kbuf[64:128, col:col + TT], TT, TT * tt),
                        "dv")
                elif kind == "q":
                    E.wait(nc.vector, "pe", mm_done[par])
                    col = ic * TOK + TT * tt
                    epi_done[par] = E.inc(
                        emit_rope(E, nc, psq[par], cos, sin, t1, t2,
                                  q_sb[0:64, col:col + TT],
                                  q_sb[64:128, col:col + TT], TT, TT * tt),
                        "dv")
                else:
                    E.wait(nc.scalar, "pe", mm_done[par])
                    E.wait(nc.scalar, "pe", last_tr)
                    ac1 = E.inc(nc.scalar.copy(vstage[:], psq[par][:]), "ac")
                    E.wait(nc.tensor, "ac", ac1)
                    E.wait(nc.tensor, "dv", last_vt)
                    for m in range(8):
                        inst = nc.tensor.transpose(
                            pst[:, 128 * m:128 * (m + 1)],
                            vstage[:, 128 * m:128 * (m + 1)], sb["eye"][:])
                    last_tr = E.inc(inst, "pe")
                    E.wait(nc.vector, "pe", last_tr)
                    for m in range(8):
                        inst = nc.vector.tensor_copy(
                            vtbuf[:, m * 512 + ic * 128:m * 512 + (ic + 1) * 128],
                            pst[:, 128 * m:128 * (m + 1)])
                    last_vt = E.inc(inst, "dv")
                    epi_done[par] = last_vt
    nc.all_engine_barrier()
    return E.c["cc"]


def emit_attention(E, ctx, sb, kall, vall, q_sb, o_dram, masks_dram, cc_done):
    nc = E.nc
    E.store_fence()
    with ExitStack() as ph:
        e = ph.enter_context
        kfm = [e(nc.sbuf_tensor(E.u("at_k"), [128, 64 * 128], BF16)) for i in range(2)]
        vtm = [e(nc.sbuf_tensor(E.u("at_v"), [128, 64 * 128], BF16)) for i in range(2)]
        masks = e(nc.sbuf_tensor(E.u("at_m"), [128, 16 * 512], BF16))
        DEPTH = 4
        LA = 3
        ptile = [e(nc.sbuf_tensor(E.u("at_p"), [128, 512], BF16)) for i in range(DEPTH)]
        pt2 = [e(nc.sbuf_tensor(E.u("at_q"), [128, 512], BF16)) for i in range(DEPTH)]
        rrow = [e(nc.sbuf_tensor(E.u("at_r"), [1, 512], F32)) for i in range(2)]
        bcsb = [e(nc.sbuf_tensor(E.u("at_b"), [128, 512], F32)) for i in range(2)]
        onorm = [e(nc.sbuf_tensor(E.u("at_o"), [128, 512], BF16)) for i in range(2)]
        pss = [e(nc.psum_tensor(E.u("at_ps"), [128, 512], F32)) for i in range(DEPTH)]
        pso = [e(nc.psum_tensor(E.u("at_po"), [128, 512], F32)) for i in range(2)]
        psl = e(nc.psum_tensor(E.u("at_pl"), [1, 512], F32))
        psb = e(nc.psum_tensor(E.u("at_pb"), [128, 512], F32))

        mask_ld = E.dma(nc.sync, masks[:].rearrange("p (m n) -> p m n", m=16),
                        masks_dram.ap().rearrange("m p n -> p m n"), "xc")
        E.wait(nc.vector, "xc", mask_ld)

        blk = 0
        norm_done = [0, 0]
        store_done = [0, 0]
        recip_done_last = 0
        pb_done = [0, 0]
        bc_copy_done = 0
        kv_ld = [0] * NKV
        pe_end = [0] * NKV
        pe_last_kv = 0
        prev_exp_last = 0
        prev_pv_last = 0

        def load_kv(kh, gate_pe):
            sem = f"xs{kh % 2}"
            E.wait(nc.scalar, "cc", cc_done)
            E.wait(nc.scalar, "pe", gate_pe)
            for rp in range(4):
                for half in range(2):
                    E.dma(nc.scalar,
                          kfm[kh % 2][:].rearrange("p (m rp d) -> p rp m d", rp=4, d=128)
                          [:, rp, 8 * half:8 * (half + 1)],
                          kall[half].ap()[512 * rp + 128 * kh:512 * rp + 128 * (kh + 1), :]
                          .rearrange("p (m d) -> p m d", d=128),
                          sem)
                    kv_ld[kh] = E.dma(
                        nc.scalar,
                        vtm[kh % 2][:].rearrange("p (m rp d) -> p rp m d", rp=4, d=128)
                        [:, rp, 8 * half:8 * (half + 1)],
                        vall[half].ap()[:, 128 * kh:128 * (kh + 1)]
                        .rearrange("(rp m p) d -> p rp m d", rp=4, p=128)[:, rp],
                        sem)

        load_kv(0, 0)
        load_kv(1, 0)
        for kh in range(NKV):
            kfm_c = kfm[kh % 2]
            vtm_c = vtm[kh % 2]
            first_of_kh = [True]
            for qh4 in range(4):
                h = 4 * kh + qh4
                for a in range(4):
                    bp = blk % 2
                    nch = 16 * a + 16
                    exp_done = [0] * nch
                    s_done = [0] * nch
                    pv_done = [0] * nch
                    msk_done = [0] * nch

                    def emit_S(c, h=h, a=a, blk=blk, exp_done=exp_done,
                               s_done=s_done, pv_done=pv_done, msk_done=msk_done):
                        sp = c % DEPTH
                        if first_of_kh[0]:
                            E.wait(nc.tensor, f"xs{kh % 2}", kv_ld[kh])
                            first_of_kh[0] = False
                        if c >= DEPTH:
                            if c % 2 == 0:
                                E.wait(nc.tensor, "ac", exp_done[c - LA])
                        elif blk > 0 and c == 0:
                            E.wait(nc.tensor, "ac", prev_exp_last)
                        s_done[c] = E.inc(
                            nc.tensor.matmul(pss[sp][:],
                                             kfm_c[:, 128 * c:128 * (c + 1)],
                                             q_sb[:, h * TOK + 512 * a:h * TOK + 512 * (a + 1)],
                                             start=True, stop=True), "pe")
                        E.wait(nc.scalar, "pe", s_done[c])
                        exp_done[c] = E.inc(
                            nc.scalar.activation(out=ptile[sp][:], in_=pss[sp][:],
                                                 func=AF.Exp, scale=INV_SQRT_HD), "ac")
                        if c >= 16 * a:
                            mi = c - 16 * a
                            E.wait(nc.vector, "ac", exp_done[c])
                            if c >= DEPTH:
                                E.wait(nc.vector, "pe", pv_done[c - DEPTH])
                            elif blk > 0:
                                E.wait(nc.vector, "pe", prev_pv_last)
                            msk_done[c] = E.inc(
                                nc.vector.tensor_mul(pt2[sp][:], ptile[sp][:],
                                                     masks[:, 512 * mi:512 * (mi + 1)]),
                                "dv")

                    def emit_PV(c, a=a, bp=bp, nch=nch, exp_done=exp_done,
                                pv_done=pv_done, msk_done=msk_done):
                        sp = c % DEPTH
                        if c >= 16 * a:
                            if c % 2 == 0 or c == 16 * a:
                                E.wait(nc.tensor, "dv",
                                       msk_done[min(c + 1, nch - 1)]
                                       if c % 2 == 0 and c + 1 >= 16 * a and c + 1 < nch
                                       else msk_done[c])
                            rhs = pt2[sp]
                        else:
                            if c % 2 == 0:
                                E.wait(nc.tensor, "ac",
                                       exp_done[min(c + 1, 16 * a - 1, nch - 1)])
                            rhs = ptile[sp]
                        if c == 0:
                            E.wait(nc.tensor, "dv",
                                   max(norm_done[bp], recip_done_last))
                        nc.tensor.matmul(pso[bp][:], vtm_c[:, 128 * c:128 * (c + 1)],
                                         rhs[:], start=(c == 0), stop=(c == nch - 1))
                        pv_done[c] = E.inc(
                            nc.tensor.matmul(psl[0:1, :], sb["ones_b"][:],
                                             rhs[:], start=(c == 0), stop=(c == nch - 1)),
                            "pe")

                    for c in range(nch):
                        emit_S(c)
                        if c == LA - 1 and E.pending_tail is not None:
                            E.pending_tail()
                            E.pending_tail = None
                        if c >= LA:
                            emit_PV(c - LA)
                    for c in range(nch - LA, nch):
                        emit_PV(c)
                    prev_exp_last = exp_done[nch - 1]
                    prev_pv_last = pv_done[nch - 1]
                    pe_last_kv = pv_done[nch - 1]
                    pe_end[kh] = pv_done[nch - 1]
                    E.wait(nc.vector, "pe", pv_done[nch - 1])
                    E.wait(nc.vector, "pe", pb_done[bp])
                    recip_done_last = E.inc(
                        nc.vector.reciprocal(rrow[bp][:], psl[0:1, :]), "dv")
                    _rd = recip_done_last
                    _bp, _h, _a, _blk = bp, h, a, blk

                    def tail(bp=_bp, h=_h, a=_a, blk=_blk, rd=_rd):
                        nonlocal bc_copy_done
                        E.wait(nc.tensor, "dv", rd)
                        E.wait(nc.tensor, "ac", bc_copy_done)
                        pb1 = E.inc(
                            nc.tensor.matmul(psb[:], sb["ones_r"][:], rrow[bp][:],
                                             start=True, stop=True), "pe")
                        pb_done[bp] = pb1
                        E.wait(nc.scalar, "pe", pb1)
                        if blk >= 2:
                            E.wait(nc.scalar, "dv", norm_done[bp])
                        bc_copy_done = E.inc(nc.scalar.copy(bcsb[bp][:], psb[:]), "ac")
                        E.wait(nc.vector, "ac", bc_copy_done)
                        if blk >= 2:
                            E.wait(nc.vector, "st", E.c["st"])
                        norm_done[bp] = E.inc(
                            nc.vector.tensor_mul(onorm[bp][:], pso[bp][:],
                                                 bcsb[bp][:]), "dv")
                        E.wait(nc.sync, "dv", norm_done[bp])
                        store_done[bp] = E.dma(
                            nc.sync,
                            o_dram.ap()[128 * h:128 * (h + 1), 512 * a:512 * (a + 1)],
                            onorm[bp][:], "st")

                    E.pending_tail = tail
                    blk += 1
            if kh + 2 < NKV:
                load_kv(kh + 2, pe_end[kh])
        if E.pending_tail is not None:
            E.pending_tail()
            E.pending_tail = None
    nc.all_engine_barrier()


def emit_wo(E, ctx, sb, wo, o_dram, x1, x_dst):
    nc = E.nc
    E.store_fence()
    with ExitStack() as ph:
        e = ph.enter_context
        wo_sb = e(nc.sbuf_tensor(E.u("wo_w"), [128, NH * D], BF16))
        oa = [e(nc.sbuf_tensor(E.u("wo_o"), [128, NH * 512], BF16)) for i in range(2)]
        x1st = [e(nc.sbuf_tensor(E.u("wo_x"), [128, NDC * 512], F32)) for i in range(2)]
        x2sb = [e(nc.sbuf_tensor(E.u("wo_y"), [128, 512], F32)) for i in range(4)]
        psw = [e(nc.psum_tensor(E.u("wo_ps"), [128, 512], F32)) for i in range(4)]
        wl_ic = [0] * NH
        for ic in range(NH):
            wl_ic[ic] = E.dma(nc.gpsimd,
                              wo_sb[:, D * ic:D * (ic + 1)],
                              wo.ap()[128 * ic:128 * (ic + 1), :], "wg0")
        wl = wl_ic[NH - 1]
        ev_done = [0] * 64
        st_done = [0] * 64
        oa_ld = [0, 0]
        x1_ld = [0, 0]
        mm_done_last = [0, 0]
        ev_done_last = [0, 0]
        idx = 0
        for a in range(4):
            apar = a % 2
            if a >= 2:
                E.wait(nc.scalar, "pe", mm_done_last[apar])
                E.wait(nc.scalar, "dv", ev_done_last[apar])
            oa_ld[apar] = E.dma(
                nc.scalar,
                oa[apar][:].rearrange("p (ic t) -> p ic t", ic=NH),
                o_dram.ap().rearrange("(ic p) t -> p ic t", p=128)[:, :, 512 * a:512 * (a + 1)],
                f"xs{apar}")
            x1_ld[apar] = E.dma(
                nc.scalar,
                x1st[apar][:].rearrange("p (dc t) -> p dc t", dc=NDC),
                x1.ap().rearrange("(dc p) t -> p dc t", p=128)[:, :, 512 * a:512 * (a + 1)],
                f"xs{apar}")
            for dd in range(NDC):
                par = idx % 4
                if idx > 0:
                    E.wait(nc.tensor, "wg0", wl)
                E.wait(nc.tensor, f"xs{apar}", x1_ld[apar])
                if idx >= 4:
                    E.wait(nc.tensor, "dv", ev_done[idx - 4])
                for ic in range(NH):
                    if idx == 0:
                        E.wait(nc.tensor, "wg0", wl_ic[ic])
                    inst = nc.tensor.matmul(
                        psw[par][:],
                        wo_sb[:, D * ic + 128 * dd:D * ic + 128 * (dd + 1)],
                        oa[apar][:, 512 * ic:512 * (ic + 1)],
                        start=(ic == 0), stop=(ic == NH - 1))
                mm = E.inc(inst, "pe")
                mm_done_last[apar] = mm
                E.wait(nc.vector, "pe", mm)
                E.wait(nc.vector, f"xs{apar}", x1_ld[apar])
                if idx >= 4:
                    E.wait(nc.vector, "st", E.c["st"])
                ev_done[idx] = E.inc(
                    nc.vector.tensor_add(x2sb[par][:], psw[par][:],
                                         x1st[apar][:, 512 * dd:512 * (dd + 1)]),
                    "dv")
                ev_done_last[apar] = ev_done[idx]
                E.wait(nc.sync, "dv", ev_done[idx])
                st_done[idx] = E.dma(
                    nc.sync,
                    x_dst.ap()[128 * dd:128 * (dd + 1), 512 * a:512 * (a + 1)],
                    x2sb[par][:], "st")
                idx += 1
    nc.all_engine_barrier()


def build_program(stage="full"):
    nc = bass.Bass("TRN2", target_bir_lowering=False)

    xfm = nc.dram_tensor("xfm", [D, TOK], F32, kind="ExternalInput")
    wq = nc.dram_tensor("wq", [D, D], F32, kind="ExternalInput")
    wk = nc.dram_tensor("wk", [D, 512], F32, kind="ExternalInput")
    wv = nc.dram_tensor("wv", [D, 512], F32, kind="ExternalInput")
    wo = nc.dram_tensor("wo", [D, D], F32, kind="ExternalInput")
    wgt = nc.dram_tensor("wgt", [D, HID], F32, kind="ExternalInput")
    wut = nc.dram_tensor("wut", [D, HID], F32, kind="ExternalInput")
    wdt = nc.dram_tensor("wdt", [HID, D], F32, kind="ExternalInput")
    cost = nc.dram_tensor("cost", [128, TOK], F32, kind="ExternalInput")
    sint = nc.dram_tensor("sint", [128, TOK], F32, kind="ExternalInput")
    masks_dram = nc.dram_tensor("masks", [16, 128, 512], BF16, kind="ExternalInput")

    out = nc.dram_tensor("out", [D, TOK], F32, kind="ExternalOutput")

    cache_g = nc.dram_tensor("cwg", [NHC, 128, D], BF16, kind="Internal")
    cache_u = nc.dram_tensor("cwu", [NHC, 128, D], BF16, kind="Internal")
    cache_d = nc.dram_tensor("cwd", [4, NHC // 2, 128, 1024], BF16, kind="Internal")
    x1d = nc.dram_tensor("x1d", [D, TOK], F32, kind="Internal")
    x2d = nc.dram_tensor("x2d", [D, TOK], F32, kind="Internal")
    kloc = [nc.dram_tensor(f"kloc{i}", [512, TT], BF16, kind="Internal")
            for i in range(2)]
    vloc = [nc.dram_tensor(f"vloc{i}", [TT, 512], BF16, kind="Internal")
            for i in range(2)]
    kall = [nc.dram_tensor(f"kall{i}", [4 * 512, TT], BF16, kind="Internal")
            for i in range(2)]
    vall = [nc.dram_tensor(f"vall{i}", [4 * TT, 512], BF16, kind="Internal")
            for i in range(2)]
    o_dram = nc.dram_tensor("o_d", [D, TOK], BF16, kind="Internal")

    eye_np = np.eye(128, dtype=ml_dtypes.bfloat16)
    ones_f_np = np.ones((128, 1), dtype=np.float32)
    ones_b_np = np.ones((128, 1), dtype=ml_dtypes.bfloat16)
    ones_r_np = np.ones((1, 128), dtype=np.float32)
    eye_d = nc.inline_tensor(eye_np, "eye_c")
    ones_f_d = nc.inline_tensor(ones_f_np, "onesf_c")
    ones_b_d = nc.inline_tensor(ones_b_np, "onesb_c")
    ones_r_d = nc.inline_tensor(ones_r_np, "onesr_c")

    with ExitStack() as ctx:
        e = ctx.enter_context
        E = Emit(nc, ctx, ["wg0", "wg1", "wg2", "wg3",
                           "xs0", "xs1", "xs2", "xs3", "xc",
                           "wc0", "wc1", "wc2", "wc3",
                           "pe", "ac", "dv", "st", "cc"])
        sb = {}
        sb["eye"] = e(nc.sbuf_tensor(E.u("c_eye"), [128, 128], BF16))
        sb["ones_f"] = e(nc.sbuf_tensor(E.u("c_onef"), [128, 1], F32))
        sb["ones_b"] = e(nc.sbuf_tensor(E.u("c_oneb"), [128, 1], BF16))
        sb["ones_r"] = e(nc.sbuf_tensor(E.u("c_oner"), [1, 128], F32))
        sb["arow"] = e(nc.sbuf_tensor(E.u("c_arow"), [1, TOK], F32))
        sb["brow"] = e(nc.sbuf_tensor(E.u("c_brow"), [1, TOK], F32))


        E.dma(nc.sync, sb["eye"][:], eye_d.ap(), "xc")
        E.dma(nc.sync, sb["ones_f"][:], ones_f_d.ap(), "xc")
        E.dma(nc.sync, sb["ones_b"][:], ones_b_d.ap(), "xc")
        cl = E.dma(nc.sync, sb["ones_r"][:], ones_r_d.ap(), "xc")
        E.wait(nc.tensor, "xc", cl)
        E.wait(nc.vector, "xc", cl)
        E.wait(nc.scalar, "xc", cl)
        nc.all_engine_barrier()

        emit_stats(E, ctx, sb, xfm)
        ffn1_dst = out if stage == "ffn1" else x1d
        emit_ffn(E, ctx, sb, xfm, ffn1_dst, wgt, wut, wdt,
                 (cache_g, cache_u, cache_d))
        if stage == "ffn1":
            E.wait(nc.sync, "st", E.c["st"])
            return nc

        emit_stats(E, ctx, sb, x1d)
        with ExitStack() as big:
            q_sb = big.enter_context(nc.sbuf_tensor(E.u("q_sb"), [128, NH * TOK], BF16))
            with ExitStack() as qk:
                cos = qk.enter_context(nc.sbuf_tensor(E.u("c_cos"), [128, TOK], F32))
                sin = qk.enter_context(nc.sbuf_tensor(E.u("c_sin"), [128, TOK], F32))
                E.dma(nc.sync, cos[:], cost.ap(), "xc")
                csl = E.dma(nc.sync, sin[:], sint.ap(), "xc")
                E.wait(nc.vector, "xc", csl)
                cc_done = emit_qkv(E, ctx, sb, x1d, wq, wk, wv, kloc, vloc,
                                   kall, vall, q_sb, cos, sin)
            emit_attention(E, ctx, sb, kall, vall, q_sb, o_dram, masks_dram,
                           cc_done)
        wo_dst = out if stage == "x2" else x2d
        emit_wo(E, ctx, sb, wo, o_dram, x1d, wo_dst)
        if stage == "x2":
            E.wait(nc.sync, "st", E.c["st"])
            return nc

        emit_stats(E, ctx, sb, x2d)
        emit_ffn(E, ctx, sb, x2d, out, wgt, wut, wdt,
                 (cache_g, cache_u, cache_d))
        E.wait(nc.sync, "st", E.c["st"])
    return nc


def prepare_inmaps(inputs):
    x = np.ascontiguousarray(np.asarray(inputs["L0_attn_x"], dtype=np.float32))
    wq = np.ascontiguousarray(np.asarray(inputs["L0_attn_Wq"], dtype=np.float32))
    wk = np.ascontiguousarray(np.asarray(inputs["L0_attn_Wk"], dtype=np.float32))
    wv = np.ascontiguousarray(np.asarray(inputs["L0_attn_Wv"], dtype=np.float32))
    wo = np.ascontiguousarray(np.asarray(inputs["L0_attn_Wo"], dtype=np.float32))
    wgt = np.ascontiguousarray(np.asarray(inputs["L0_ffn_w_gate"], dtype=np.float32).T)
    wut = np.ascontiguousarray(np.asarray(inputs["L0_ffn_w_up"], dtype=np.float32).T)
    wdt = np.ascontiguousarray(np.asarray(inputs["L0_ffn_w_down"], dtype=np.float32).T)

    inv = (1.0 / 10000.0 ** (np.arange(0, HD, 2, dtype=np.float64) / HD))

    in_maps = []
    idx_all = []
    for c in range(NCORES):
        b, r = c // 4, c % 4
        m = np.arange(16)
        i = np.arange(128)
        tpos = (128 * (4 * m[:, None] + r) + i[None, :]).reshape(-1)
        idx_all.append((b, tpos))
        xfm = np.ascontiguousarray(x[b][tpos].T)
        f = tpos[None, :].astype(np.float64) * inv[:, None]
        cos_t = np.concatenate([np.cos(f), np.cos(f)], axis=0).astype(np.float32)
        sin_t = np.concatenate([np.sin(f), np.sin(f)], axis=0).astype(np.float32)
        masks = np.zeros((16, 128, 512), dtype=np.float32)
        stair = (np.arange(128)[None, :] >= np.arange(128)[:, None]).astype(np.float32)
        for mi in range(16):
            for s in range(4):
                g = 4 * s + r
                if mi < g:
                    masks[mi, :, 128 * s:128 * (s + 1)] = 1.0
                elif mi == g:
                    masks[mi, :, 128 * s:128 * (s + 1)] = stair
        in_maps.append({
            "xfm": xfm, "wq": wq, "wk": wk, "wv": wv, "wo": wo,
            "wgt": wgt, "wut": wut, "wdt": wdt,
            "cost": cos_t, "sint": sin_t,
            "masks": masks.astype(ml_dtypes.bfloat16),
        })
    return in_maps, idx_all


def kernel(**inputs):
    in_maps, idx_all = prepare_inmaps(inputs)
    nc = build_program("full")
    res = run_bass_kernel_spmd(nc, in_maps, core_ids=list(range(NCORES)))
    out = np.empty((B, T, D), dtype=np.float32)
    for c in range(NCORES):
        b, tpos = idx_all[c]
        out[b][tpos] = np.asarray(res.results[c]["out"], dtype=np.float32).T
    return out
